# version 01
# baseline (speedup 1.0000x reference)
import sys
if '/opt/trn_rl_repo' not in sys.path:
    sys.path.insert(0, '/opt/trn_rl_repo')
from contextlib import ExitStack
import numpy as np
import ml_dtypes
import concourse.bass as bass
import concourse.mybir as mybir
from concourse.bass_utils import run_bass_kernel_spmd

F32 = mybir.dt.float32
BF16 = mybir.dt.bfloat16
I32 = mybir.dt.int32
AF = mybir.ActivationFunctionType
ALU = mybir.AluOpType
AXL = mybir.AxisListType

B, S, D = 4, 4096, 1024
NCORES = 8
TPC = B * S // NCORES
DI = 2048
NST = 16
CH = 512
NCHUNK = TPC // CH
EPS = 1e-5

ENG = ("tensor", "scalar", "vector", "gpsimd", "sync")
DMAQ = ("sync", "scalar", "gpsimd")
BF = ml_dtypes.bfloat16


class Prog:

    def __init__(self):
        self.nc = bass.Bass("TRN2", target_bir_lowering=False)
        self.recs = {e: [] for e in ENG}
        self.cnt = {e: 0 for e in ENG}
        self.dma_cnt = {q: 0 for q in DMAQ}
        self.last_w = {}
        self.readers = {}
        self.mark = {}
        self.esem = {e: self.nc.alloc_semaphore(f"sem_{e}") for e in ENG}
        self.NSLOT = 8
        self.dsem = {(q, i): self.nc.alloc_semaphore(f"dma_{q}{i}")
                     for q in DMAQ for i in range(self.NSLOT)}
        self._const_cols = {}

    def _wait_for(self, eng, waits, src):
        import os
        if src[0] == 'e':
            _, e2, idx = src
            key, val, sem = (eng, ('e', e2)), idx + 1, self.esem[e2]
        else:
            _, q, slot, use = src
            key, val, sem = (eng, ('d', q, slot)), 16 * use, self.dsem[(q, slot)]
        if os.environ.get("NO_DEDUP") != "1" and self.mark.get(key, 0) >= val:
            return
        self.mark[key] = val
        if (sem, val) not in waits:
            waits.append((sem, val))

    def _deps(self, eng, reads, writes):
        waits = []
        for r in reads:
            src = self.last_w.get(r)
            if src is not None:
                self._wait_for(eng, waits, src)
        for w in writes:
            src = self.last_w.get(w)
            if src is not None:
                self._wait_for(eng, waits, src)
            for rd in self.readers.get(w, ()):
                self._wait_for(eng, waits, rd)
        return waits

    def _commit(self, src, reads, writes):
        for r in reads:
            self.readers.setdefault(r, []).append(src)
        for w in writes:
            self.last_w[w] = src
            self.readers[w] = []

    @staticmethod
    def _norm(regions):
        out = []
        for r in regions:
            if isinstance(r, (str, tuple, int)):
                out.append(r)
            elif hasattr(r, "tensor"):
                out.append(r.tensor.name)
            elif hasattr(r, "name"):
                out.append(r.name)
            else:
                out.append(id(r))
        return out

    def op(self, eng, fn, reads=(), writes=()):
        reads, writes = self._norm(reads), self._norm(writes)
        waits = self._deps(eng, reads, writes)
        idx = self.cnt[eng]
        self.cnt[eng] += 1
        self.recs[eng].append(("op", fn, waits, None))
        self._commit(('e', eng, idx), reads, writes)

    def dma(self, q, out, in_, reads=(), writes=()):
        reads, writes = self._norm(reads), self._norm(writes)
        waits = self._deps(q, reads, writes)
        idx = self.dma_cnt[q]
        self.dma_cnt[q] = idx + 1
        slot = idx % self.NSLOT
        use = idx // self.NSLOT + 1
        if use > 1:
            self._wait_for(q, waits, ('d', q, slot, use - 1))
        self.recs[q].append(("dma", (out, in_, slot), waits, q))
        self._commit(('d', q, slot, use), reads, writes)

    def barrier(self):
        snap_e = dict(self.cnt)
        snap_d = dict(self.dma_cnt)
        for e in ENG:
            waits = []
            for e2 in ENG:
                if e2 != e and snap_e[e2] > 0:
                    self._wait_for(e, waits, ('e', e2, snap_e[e2] - 1))
            for q in DMAQ:
                n = snap_d[q]
                for slot in range(self.NSLOT):
                    use = (n - slot + self.NSLOT - 1) // self.NSLOT
                    if use > 0:
                        self._wait_for(e, waits, ('d', q, slot, use))
            if waits:
                self.recs[e].append(("waits", None, waits, None))

    def finish(self):
        waits = []
        for q in DMAQ:
            n = self.dma_cnt[q]
            for slot in range(self.NSLOT):
                use = (n - slot + self.NSLOT - 1) // self.NSLOT
                if use > 0:
                    self._wait_for("sync", waits, ('d', q, slot, use))
        if waits:
            self.recs["sync"].append(("waits", None, waits, None))

    def emit(self):
        nc = self.nc
        with nc.Block() as blk:
            for ename, deco in (("tensor", blk.tensor), ("scalar", blk.scalar),
                                ("vector", blk.vector), ("gpsimd", blk.gpsimd),
                                ("sync", blk.sync)):
                recs = self.recs[ename]
                if not recs:
                    continue
                esem = self.esem[ename]

                def body(engine, recs=recs, esem=esem):
                    for kind, pay, waits, q in recs:
                        for sem, val in waits:
                            engine.wait_ge(sem, val)
                        if kind == "op":
                            pay(engine).then_inc(esem, 1)
                        elif kind == "dma":
                            out, in_, slot = pay
                            engine.dma_start(out=out, in_=in_).then_inc(self.dsem[(q, slot)], 16)
                deco(body)
        return nc

    def sb(self, name, shape, dtype):
        return self.nc.alloc_sbuf_tensor(name, shape, dtype)

    def ps(self, name, cols=512, dtype=F32):
        return self.nc.alloc_psum_tensor(name, [128, cols], dtype)

    def const_col(self, value, rows=128):
        key = (value, rows)
        if key not in self._const_cols:
            t = self.sb(f"cc_{len(self._const_cols)}", [rows, 1], F32)
            self.op("vector", lambda v, t=t, value=value: v.memset(t[:, :], value), writes=[t])
            self._const_cols[key] = t
        return self._const_cols[key]

    def ln_cols(self, x_ap, x_reg, F, st, c0, sq_tmp, eps_col):
        m = st[:, c0:c0 + 1]
        r = st[:, c0 + 1:c0 + 2]
        t2 = st[:, c0 + 2:c0 + 3]
        t3 = st[:, c0 + 3:c0 + 4]
        P = x_ap.shape[0]
        nh = (F + CH - 1) // CH
        for h in range(nh):
            f0 = h * CH
            f1 = min(F, f0 + CH)
            acc = r if h == 0 else t3
            self.op("scalar", lambda s, f0=f0, f1=f1, acc=acc: s.activation(
                sq_tmp[0:P, 0:f1 - f0], x_ap[:, f0:f1], AF.Square, accum_out=acc),
                reads=[x_reg], writes=[sq_tmp, st])
            if h > 0:
                self.op("vector", lambda v: v.tensor_tensor(r, r, t3, op=ALU.add),
                        reads=[st], writes=[st])
        self.op("vector", lambda v: v.reduce_sum(m, x_ap, axis=AXL.X), reads=[x_reg], writes=[st])
        inv = 1.0 / F
        self.op("vector", lambda v: v.tensor_single_scalar(m, m, inv, op=ALU.mult), reads=[st], writes=[st])
        self.op("vector", lambda v: v.tensor_single_scalar(r, r, inv, op=ALU.mult), reads=[st], writes=[st])
        self.op("vector", lambda v: v.tensor_tensor(t2, m, m, op=ALU.mult), reads=[st], writes=[st])
        self.op("vector", lambda v: v.tensor_tensor(r, r, t2, op=ALU.subtract), reads=[st], writes=[st])
        self.op("scalar", lambda s: s.activation(r, r, AF.Sqrt, bias=eps_col[0:r.shape[0], 0:1]),
                reads=[st, eps_col], writes=[st])
        self.op("vector", lambda v: v.reciprocal(r, r), reads=[st], writes=[st])
        return m, r


def build_launch1():
    p = Prog()
    nc = p.nc

    x_in = nc.dram_tensor("x", [TPC + 3, D], F32, kind="ExternalInput").ap()
    w_inT = nc.dram_tensor("w_inT", [D, 2 * DI], F32, kind="ExternalInput").ap()
    lnw_rep = nc.dram_tensor("lnw_rep", [128, D], F32, kind="ExternalInput").ap()
    lnb_rep = nc.dram_tensor("lnb_rep", [128, D], F32, kind="ExternalInput").ap()
    convw = nc.dram_tensor("convw", [128, 64], F32, kind="ExternalInput").ap()
    convb = nc.dram_tensor("convb", [128, 16], F32, kind="ExternalInput").ap()
    wxT = nc.dram_tensor("wxT", [DI, 128], F32, kind="ExternalInput").ap()
    wdtT = nc.dram_tensor("wdtT", [64, DI], F32, kind="ExternalInput").ap()
    bdt = nc.dram_tensor("bdt", [128, 16], F32, kind="ExternalInput").ap()
    sgT = nc.dram_tensor("sgT", [D, DI], F32, kind="ExternalInput").ap()
    suT = nc.dram_tensor("suT", [D, DI], F32, kind="ExternalInput").ap()
    sdT = nc.dram_tensor("sdT", [DI, D], F32, kind="ExternalInput").ap()
    rtT = nc.dram_tensor("rtT", [D, 8], F32, kind="ExternalInput").ap()
    rbias = nc.dram_tensor("rbias", [8, 1], F32, kind="ExternalInput").ap()
    wgT = nc.dram_tensor("wgT", [8, D, DI], F32, kind="ExternalInput").ap()
    wuT = nc.dram_tensor("wuT", [8, D, DI], F32, kind="ExternalInput").ap()
    wdT = nc.dram_tensor("wdT", [8, DI, D], F32, kind="ExternalInput").ap()
    id_bf = nc.dram_tensor("id_bf", [128, 128], BF16, kind="ExternalInput").ap()
    id_f32 = nc.dram_tensor("id_f32", [128, 128], F32, kind="ExternalInput").ap()
    sel8 = nc.dram_tensor("sel8", [8, 8 * 128], BF16, kind="ExternalInput").ap()
    hmask = nc.dram_tensor("hmask", [3, 1], F32, kind="ExternalInput").ap()

    dt_out = nc.dram_tensor("dt", [128, 16, TPC], BF16, kind="ExternalOutput").ap()
    xconv_out = nc.dram_tensor("xconv", [128, 16, TPC], BF16, kind="ExternalOutput").ap()
    zs_out = nc.dram_tensor("zsilu", [128, 16, TPC], BF16, kind="ExternalOutput").ap()
    bmcm_out = nc.dram_tensor("bmcm", [64, TPC], F32, kind="ExternalOutput").ap()
    gpre_out = nc.dram_tensor("gpre", [128, 8, TPC], F32, kind="ExternalOutput").ap()
    dbg_out = nc.dram_tensor("dbg", [8, TPC], BF16, kind="ExternalOutput").ap()
    dbg2_out = nc.dram_tensor("dbg2", [8, CH], F32, kind="ExternalOutput").ap()
    dbg3_out = nc.dram_tensor("dbg3", [128, 32], F32, kind="ExternalOutput").ap()
    dbg4_out = nc.dram_tensor("dbg4", [128, 8], BF16, kind="ExternalOutput").ap()
    dbg5_out = nc.dram_tensor("dbg5", [8, 128], BF16, kind="ExternalOutput").ap()

    winb = [p.sb(f"winb{k}", [128, 2 * DI], BF16) for k in range(8)]
    lnw_sb = p.sb("lnw_sb", [128, D], BF16)
    lnb_sb = p.sb("lnb_sb", [128, D], BF16)
    convw_sb = p.sb("convw_sb", [128, 64], F32)
    convb_sb = p.sb("convb_sb", [128, 16], F32)
    wx_sb = p.sb("wx_sb", [128, 16 * 128], BF16)
    wdt_sb = p.sb("wdt_sb", [64, DI], BF16)
    bdt_sb = p.sb("bdt_sb", [128, 16], F32)
    rt_sb = p.sb("rt_sb", [128, 64], BF16)
    rb_sb = p.sb("rb_sb", [8, 1], F32)
    id_bf_sb = p.sb("id_bf_sb", [128, 128], BF16)
    id_f32_sb = p.sb("id_f32_sb", [128, 128], F32)
    stats = p.sb("stats", [128, 16], F32)
    sq_tmp = p.sb("sq_tmp", [128, CH], F32)
    xn0_fm = p.sb("xn0_fm", [128, 8, TPC], BF16)
    w_tm = p.sb("w_tm", [128, 16, 8], F32)

    ps_mm = p.ps("ps_mm")
    ps_u = p.ps("ps_u")
    ps_y = p.ps("ps_y")
    ps_w = p.ps("ps_w")
    ps_tb = nc.alloc_psum_tensor("ps_tb", [128, 1024], BF16)
    ps_tf = p.ps("ps_tf")

    for k in range(8):
        p.dma("gpsimd", winb[k][:, :], w_inT[k * 128:(k + 1) * 128, :], writes=[winb[k]])
    p.dma("gpsimd", lnw_sb[:, :], lnw_rep, writes=[lnw_sb])
    p.dma("gpsimd", lnb_sb[:, :], lnb_rep, writes=[lnb_sb])
    p.dma("sync", convw_sb[:, :], convw, writes=[convw_sb])
    p.dma("sync", convb_sb[:, :], convb, writes=[convb_sb])
    p.dma("gpsimd", wx_sb[:, :].rearrange("p (c o) -> p c o", c=16),
          wxT.rearrange("(c p) o -> p c o", p=128), writes=[wx_sb])
    p.dma("gpsimd", wdt_sb[:, :], wdtT, writes=[wdt_sb])
    p.dma("sync", bdt_sb[:, :], bdt, writes=[bdt_sb])
    p.dma("gpsimd", rt_sb[:, :].rearrange("p (c o) -> p c o", c=8),
          rtT.rearrange("(c p) o -> p c o", p=128), writes=[rt_sb])
    p.dma("sync", rb_sb[:, :], rbias, writes=[rb_sb])
    p.dma("sync", id_bf_sb[:, :], id_bf, writes=[id_bf_sb])
    p.dma("sync", id_f32_sb[:, :], id_f32, writes=[id_f32_sb])

    one_col = p.const_col(1.0)
    eps_col = p.const_col(EPS)
    eps3 = p.const_col(EPS, rows=3)

    with ExitStack() as stk:
        def tmp(name, shape, dtype):
            return stk.enter_context(nc.sbuf_tensor(name, shape, dtype))

        xc_buf = [tmp(f"xc{e}", [128, CH + 3], BF16) for e in range(16)]
        x_t = [tmp(f"x_t{i}", [128, D], F32) for i in range(2)]
        xn0_t = tmp("xn0_t", [128, D], F32)
        xnl_t = tmp("xnl_t", [128, D], F32)
        xn0bf_t = tmp("xn0bf_t", [128, D], BF16)
        xnlbf_t = tmp("xnlbf_t", [128, D], BF16)
        xnl_fm = tmp("xnl_fm", [128, 8, CH], BF16)
        xconv_ch = tmp("xconv_ch", [128, 16, CH], BF16)
        zs_ch = tmp("zs_ch", [128, 8, CH], BF16)
        dtr_bf = tmp("dtr_bf", [64, CH], BF16)
        bmcm_ch = tmp("bmcm_ch", [64, CH], F32)
        dt_ch = tmp("dt_ch", [128, 8, CH], BF16)
        sp_abs = tmp("sp_abs", [128, CH], F32)
        sp_exp = tmp("sp_exp", [128, CH], F32)
        sp_ln = tmp("sp_ln", [128, CH], F32)
        sp_rel = tmp("sp_rel", [128, CH], F32)
        cv_u = tmp("cv_u", [128, CH], F32)
        cv_v = tmp("cv_v", [128, CH], F32)
        lg_fm = tmp("lg_fm", [8, CH], F32)
        lg_tm = tmp("lg_tm", [128, 4, 8], F32)
        top8 = tmp("top8", [128, 8], F32)
        sig_all = tmp("sig_all", [128, 8], F32)
        sig12 = tmp("sig12", [128, 4], F32)
        mask8 = tmp("mask8", [128, 8], F32)
        hstk = ExitStack()
        def htmp(name, shape, dtype):
            return hstk.enter_context(nc.sbuf_tensor(name, shape, dtype))
        xh_full = x_t[0]
        xh_tm = xh_full[0:3, :]
        halo_st = stats[0:3, 8:16].tensor if False else stats
        halo_xn = xn0_t[0:3, :]
        halo_bf = xn0bf_t[0:3, :]
        halo_fm = htmp("halo_fm", [128, 8, 3], BF16)
        xch_sb = htmp("xch_sb", [128, 48], F32)
        halo_sq = sq_tmp

        p.dma("sync", xh_tm, x_in[0:3, :], writes=[x_t[0]])
        hmask_sb = htmp("hmask_sb", [3, 1], F32)
        p.dma("sync", hmask_sb[:, :], hmask, writes=[hmask_sb])

        m, r = p.ln_cols(xh_tm, x_t[0], D, stats[0:3, :], 0, halo_sq, eps3)
        p.op("vector", lambda v, m=m, r=r: v.tensor_scalar(halo_xn, xh_tm, m, r,
                                                 op0=ALU.subtract, op1=ALU.mult),
             reads=[x_t[0], stats], writes=[xn0_t])
        m, r = p.ln_cols(halo_xn, xn0_t, D, stats[0:3, :], 4, halo_sq, eps3)
        p.op("vector", lambda v, m=m, r=r: v.tensor_scalar(halo_xn, halo_xn, m, r,
                                                 op0=ALU.subtract, op1=ALU.mult),
             reads=[xn0_t, stats], writes=[xn0_t])
        p.op("vector", lambda v: v.tensor_tensor(halo_xn, halo_xn, lnw_sb[0:3, :], op=ALU.mult),
             reads=[xn0_t, lnw_sb], writes=[xn0_t])
        p.op("vector", lambda v: v.tensor_tensor(halo_xn, halo_xn, lnb_sb[0:3, :], op=ALU.add),
             reads=[xn0_t, lnb_sb], writes=[xn0_t])
        p.op("vector", lambda v: v.tensor_scalar(halo_xn, halo_xn, hmask_sb[:, 0:1],
                                                 None, op0=ALU.mult),
             reads=[xn0_t, hmask_sb], writes=[xn0_t])
        p.op("vector", lambda v: v.tensor_copy(halo_bf, halo_xn),
             reads=[xn0_t], writes=[xn0bf_t])
        for kd in range(8):
            p.op("tensor", lambda t, kd=kd: t.transpose(
                ps_tb[:, kd * 4:kd * 4 + 3], halo_bf[:, kd * 128:(kd + 1) * 128], id_bf_sb[0:3, 0:3]),
                reads=[xn0bf_t, id_bf_sb], writes=[ps_tb])
        p.op("vector", lambda v: v.tensor_copy(
            halo_fm[:, :, :], ps_tb[:, 0:32].rearrange("p (a b) -> p a b", b=4)[:, :, 0:3]),
            reads=[ps_tb], writes=[halo_fm])
        for e in range(16):
            for kd in range(8):
                p.op("tensor", lambda t, e=e, kd=kd: t.matmul(
                    ps_tf[:, e * 3:(e + 1) * 3], winb[kd][:, e * 128:(e + 1) * 128],
                    halo_fm[:, kd, :], start=(kd == 0), stop=(kd == 7)),
                    reads=[winb[kd], halo_fm], writes=[ps_tf])
        p.op("vector", lambda v: v.tensor_copy(xch_sb[:, 0:48], ps_tf[:, 0:48]),
             reads=[ps_tf], writes=[xch_sb])
        for e in range(16):
            p.op("vector", lambda v, e=e: v.tensor_copy(xc_buf[e][:, 0:3], xch_sb[:, e * 3:(e + 1) * 3]),
                 reads=[xch_sb], writes=[xc_buf[e]])
        hstk.close()

        for ck in range(NCHUNK):
            tok0 = ck * CH
            for tt in range(4):
                t = ck * 4 + tt
                xb = x_t[tt % 2]
                p.dma("sync", xb[:, :], x_in[3 + t * 128:3 + (t + 1) * 128, :], writes=[xb])
                m, r = p.ln_cols(xb[:, :], xb, D, stats, 0, sq_tmp, eps_col)
                p.op("vector", lambda v, xb=xb, m=m, r=r: v.tensor_scalar(xn0_t[:, :], xb[:, :], m, r,
                                                                op0=ALU.subtract, op1=ALU.mult),
                     reads=[xb, stats], writes=[xn0_t])
                m2, r2 = p.ln_cols(xn0_t[:, :], xn0_t, D, stats, 4, sq_tmp, eps_col)
                p.op("vector", lambda v, m2=m2, r2=r2: v.tensor_scalar(xnl_t[:, :], xn0_t[:, :], m2, r2,
                                                         op0=ALU.subtract, op1=ALU.mult),
                     reads=[xn0_t, stats], writes=[xnl_t])
                p.op("vector", lambda v: v.tensor_tensor(xnl_t[:, :], xnl_t[:, :], lnw_sb[:, :], op=ALU.mult),
                     reads=[xnl_t, lnw_sb], writes=[xnl_t])
                p.op("vector", lambda v: v.tensor_tensor(xnl_t[:, :], xnl_t[:, :], lnb_sb[:, :], op=ALU.add),
                     reads=[xnl_t, lnb_sb], writes=[xnl_t])
                p.op("vector", lambda v: v.tensor_copy(xn0bf_t[:, :], xn0_t[:, :]),
                     reads=[xn0_t], writes=[xn0bf_t])
                p.op("vector", lambda v: v.tensor_copy(xnlbf_t[:, :], xnl_t[:, :]),
                     reads=[xnl_t], writes=[xnlbf_t])
                for kd in range(8):
                    p.op("tensor", lambda t_, kd=kd: t_.transpose(
                        ps_tb[:, kd * 128:(kd + 1) * 128], xn0bf_t[:, kd * 128:(kd + 1) * 128],
                        id_bf_sb[:, :]),
                        reads=[xn0bf_t, id_bf_sb], writes=[ps_tb])
                p.op("vector", lambda v, t=t: v.tensor_copy(
                    xn0_fm[:, :, t * 128:(t + 1) * 128],
                    ps_tb[:, :].rearrange("p (a b) -> p a b", a=8)),
                    reads=[ps_tb], writes=[xn0_fm])
                for kd in range(8):
                    p.op("tensor", lambda t_, kd=kd: t_.transpose(
                        ps_tb[:, kd * 128:(kd + 1) * 128], xnlbf_t[:, kd * 128:(kd + 1) * 128],
                        id_bf_sb[:, :]),
                        reads=[xnlbf_t, id_bf_sb], writes=[ps_tb])
                p.op("vector", lambda v, tt=tt: v.tensor_copy(
                    xnl_fm[:, :, tt * 128:(tt + 1) * 128],
                    ps_tb[:, :].rearrange("p (a b) -> p a b", a=8)),
                    reads=[ps_tb], writes=[xnl_fm])

            for e in range(32):
                for kd in range(8):
                    p.op("tensor", lambda t, e=e, kd=kd: t.matmul(
                        ps_mm[:, :], winb[kd][:, e * 128:(e + 1) * 128], xnl_fm[:, kd, :],
                        start=(kd == 0), stop=(kd == 7)),
                        reads=[winb[kd], xnl_fm], writes=[ps_mm])
                if e < 16:
                    if ck > 0:
                        p.op("vector", lambda v, e=e: v.tensor_copy(xc_buf[e][:, 0:3], xc_buf[e][:, CH:CH + 3]),
                             reads=[xc_buf[e]], writes=[xc_buf[e]])
                    p.op("vector", lambda v, e=e: v.tensor_copy(xc_buf[e][:, 3:CH + 3], ps_mm[:, :]),
                         reads=[ps_mm], writes=[xc_buf[e]])
                else:
                    p.op("scalar", lambda s, e=e: s.activation(zs_ch[:, (e - 16) % 8, :], ps_mm[:, :], AF.Silu),
                         reads=[ps_mm], writes=[zs_ch])
                    if e == 23:
                        p.dma("sync", zs_out[:, 0:8, tok0:tok0 + CH], zs_ch[:, :, :], reads=[zs_ch])
            p.dma("sync", zs_out[:, 8:16, tok0:tok0 + CH], zs_ch[:, :, :], reads=[zs_ch])

            for e in range(16):
                p.op("vector", lambda v, e=e: v.tensor_scalar(
                    cv_u[:, :], xc_buf[e][:, 0:CH], convw_sb[:, 4 * e:4 * e + 1],
                    convb_sb[:, e:e + 1], op0=ALU.mult, op1=ALU.add),
                    reads=[xc_buf[e], convw_sb, convb_sb], writes=[cv_u])
                p.op("vector", lambda v, e=e: v.scalar_tensor_tensor(
                    cv_u[:, :], xc_buf[e][:, 1:CH + 1], convw_sb[:, 4 * e + 1:4 * e + 2], cv_u[:, :],
                    op0=ALU.mult, op1=ALU.add),
                    reads=[xc_buf[e], convw_sb, cv_u], writes=[cv_u])
                p.op("vector", lambda v, e=e: v.tensor_scalar(
                    cv_v[:, :], xc_buf[e][:, 2:CH + 2], convw_sb[:, 4 * e + 2:4 * e + 3], None,
                    op0=ALU.mult),
                    reads=[xc_buf[e], convw_sb], writes=[cv_v])
                p.op("vector", lambda v, e=e: v.scalar_tensor_tensor(
                    cv_v[:, :], xc_buf[e][:, 3:CH + 3], convw_sb[:, 4 * e + 3:4 * e + 4], cv_v[:, :],
                    op0=ALU.mult, op1=ALU.add),
                    reads=[xc_buf[e], convw_sb, cv_v], writes=[cv_v])
                p.op("vector", lambda v: v.tensor_tensor(cv_u[:, :], cv_u[:, :], cv_v[:, :], op=ALU.add),
                     reads=[cv_u, cv_v], writes=[cv_u])
                p.op("scalar", lambda s, e=e: s.activation(xconv_ch[:, e, :], cv_u[:, :], AF.Silu),
                     reads=[cv_u], writes=[xconv_ch])
            p.dma("sync", xconv_out[:, :, tok0:tok0 + CH], xconv_ch[:, :, :], reads=[xconv_ch])

            for kc in range(16):
                p.op("tensor", lambda t, kc=kc: t.matmul(
                    ps_y[:, :], wx_sb[:, kc * 128:(kc + 1) * 128], xconv_ch[:, kc, :],
                    start=(kc == 0), stop=(kc == 15)),
                    reads=[wx_sb, xconv_ch], writes=[ps_y])
            p.op("vector", lambda v: v.tensor_copy(dtr_bf[:, :], ps_y[0:64, :]),
                 reads=[ps_y], writes=[dtr_bf])
            p.op("vector", lambda v: v.tensor_copy(bmcm_ch[:, :], ps_y[64:128, :]),
                 reads=[ps_y], writes=[bmcm_ch])
            p.dma("sync", bmcm_out[:, tok0:tok0 + CH], bmcm_ch[:, :], reads=[bmcm_ch])

            for dch in range(16):
                p.op("tensor", lambda t, dch=dch: t.matmul(
                    ps_y[:, :], wdt_sb[:, dch * 128:(dch + 1) * 128], dtr_bf[:, :],
                    start=True, stop=True),
                    reads=[wdt_sb, dtr_bf], writes=[ps_y])
                p.op("scalar", lambda s, dch=dch: s.activation(sp_abs[:, :], ps_y[:, :], AF.Abs,
                                                               bias=bdt_sb[:, dch:dch + 1]),
                     reads=[ps_y, bdt_sb], writes=[sp_abs])
                p.op("scalar", lambda s, dch=dch: s.activation(sp_rel[:, :], ps_y[:, :], AF.Relu,
                                                               bias=bdt_sb[:, dch:dch + 1]),
                     reads=[ps_y, bdt_sb], writes=[sp_rel])
                p.op("scalar", lambda s: s.activation(sp_exp[:, :], sp_abs[:, :], AF.Exp, scale=-1.0),
                     reads=[sp_abs], writes=[sp_exp])
                p.op("scalar", lambda s: s.activation(sp_ln[:, :], sp_exp[:, :], AF.Ln, bias=one_col[:, 0:1]),
                     reads=[sp_exp, one_col], writes=[sp_ln])
                p.op("vector", lambda v, dch=dch: v.tensor_tensor(dt_ch[:, dch % 8, :], sp_rel[:, :],
                                                                  sp_ln[:, :], op=ALU.add),
                     reads=[sp_rel, sp_ln], writes=[dt_ch])
                if dch == 7:
                    p.dma("sync", dt_out[:, 0:8, tok0:tok0 + CH], dt_ch[:, :, :], reads=[dt_ch])
            p.dma("sync", dt_out[:, 8:16, tok0:tok0 + CH], dt_ch[:, :, :], reads=[dt_ch])

            for kd in range(8):
                p.op("tensor", lambda t, kd=kd, tok0=tok0: t.matmul(
                    ps_w[0:8, 0:CH], rt_sb[:, kd * 8:(kd + 1) * 8], xn0_fm[:, kd, tok0:tok0 + CH],
                    start=(kd == 0), stop=(kd == 7)),
                    reads=[rt_sb, xn0_fm], writes=[ps_w])
            p.op("vector", lambda v: v.tensor_scalar(lg_fm[:, :], ps_w[0:8, 0:CH], rb_sb[0:8, 0:1],
                                                     None, op0=ALU.add),
                 reads=[ps_w, rb_sb], writes=[lg_fm])
            for c4 in range(4):
                p.op("tensor", lambda t, c4=c4: t.transpose(
                    ps_tf[:, c4 * 8:(c4 + 1) * 8], lg_fm[0:8, c4 * 128:(c4 + 1) * 128],
                    id_f32_sb[0:8, 0:8]),
                    reads=[lg_fm, id_f32_sb], writes=[ps_tf])
            p.op("vector", lambda v: v.tensor_copy(
                lg_tm[:, :, :].rearrange("p a b -> p (a b)"), ps_tf[:, 0:32]),
                reads=[ps_tf], writes=[lg_tm])
            for c4 in range(4):
                t = ck * 4 + c4
                p.op("vector", lambda v, c4=c4: v.max(top8[:, :], lg_tm[:, c4, :]),
                     reads=[lg_tm], writes=[top8])
                p.op("scalar", lambda s, c4=c4: s.activation(sig_all[:, :], lg_tm[:, c4, :],
                                                             AF.Tanh, scale=0.5),
                     reads=[lg_tm], writes=[sig_all])
                p.op("vector", lambda v: v.tensor_scalar(sig_all[:, :], sig_all[:, :], 0.5, 0.5,
                                                         op0=ALU.mult, op1=ALU.add),
                     reads=[sig_all], writes=[sig_all])
                p.op("scalar", lambda s: s.activation(sig12[:, 0:2], top8[:, 0:2], AF.Tanh, scale=0.5),
                     reads=[top8], writes=[sig12])
                p.op("vector", lambda v: v.tensor_scalar(sig12[:, 0:2], sig12[:, 0:2], 0.5, 0.5,
                                                         op0=ALU.mult, op1=ALU.add),
                     reads=[sig12], writes=[sig12])
                p.op("vector", lambda v: v.tensor_tensor(sig12[:, 2:3], sig12[:, 0:1], sig12[:, 1:2],
                                                         op=ALU.add),
                     reads=[sig12], writes=[sig12])
                p.op("vector", lambda v: v.reciprocal(sig12[:, 3:4], sig12[:, 2:3]),
                     reads=[sig12], writes=[sig12])
                p.op("vector", lambda v, c4=c4: v.tensor_scalar(mask8[:, :], lg_tm[:, c4, :],
                                                                top8[:, 1:2], None, op0=ALU.is_ge),
                     reads=[lg_tm, top8], writes=[mask8])
                p.op("vector", lambda v: v.tensor_tensor(mask8[:, :], mask8[:, :], sig_all[:, :],
                                                         op=ALU.mult),
                     reads=[mask8, sig_all], writes=[mask8])
                p.op("vector", lambda v, t=t: v.tensor_scalar(w_tm[:, t, :], mask8[:, :],
                                                              sig12[:, 3:4], None, op0=ALU.mult),
                     reads=[mask8, sig12], writes=[w_tm])

        p.dma("sync", dbg2_out, lg_fm[:, :], reads=[lg_fm])
        dbg3 = tmp("dbg3_sb", [128, 32], F32)
        p.op("vector", lambda v: v.tensor_copy(dbg3[:, 0:8], top8[:, :]), reads=[top8], writes=[dbg3])
        p.op("vector", lambda v: v.tensor_copy(dbg3[:, 8:16], sig_all[:, :]), reads=[sig_all], writes=[dbg3])
        p.op("vector", lambda v: v.tensor_copy(dbg3[:, 16:24], mask8[:, :]), reads=[mask8], writes=[dbg3])
        p.op("vector", lambda v: v.tensor_copy(dbg3[:, 24:32], w_tm[:, 15, :]), reads=[w_tm], writes=[dbg3])
        p.dma("sync", dbg3_out, dbg3[:, :], reads=[dbg3])

    p.barrier()

    sel8_sb = p.sb("sel8_sb", [8, 8 * 128], BF16)
    p.dma("sync", sel8_sb[:, :], sel8, writes=[sel8_sb])
    w_fm = p.sb("w_fm", [8, TPC], BF16)
    wtm_bf = p.sb("wtm_bf", [128, 8], BF16)
    sdb = [p.sb(f"sdb{k}", [128, D], BF16) for k in range(16)]
    h_moe = p.sb("h_moe", [128, 16, CH], BF16)
    hsil = p.sb("hsil", [128, CH], BF16)
    w_rep = p.sb("w_rep", [128, CH], BF16)
    tmp_y = p.sb("tmp_y", [128, CH], F32)
    gacc = p.sb("gacc", [128, 8, TPC // 2], F32)
    sgb = [winb[k][:, 0:DI] for k in range(8)]
    sub = [winb[k][:, DI:2 * DI] for k in range(8)]
    sg_reg = [("winA", k) for k in range(8)]
    su_reg = [("winB", k) for k in range(8)]

    dbg5_sb = p.sb("dbg5_sb", [8, 128], BF16)
    for t in range(16):
        p.op("vector", lambda v, t=t: v.tensor_copy(wtm_bf[:, :], w_tm[:, t, :]),
             reads=[w_tm], writes=[wtm_bf])
        if t == 0:
            p.dma("sync", dbg4_out, wtm_bf[:, :], reads=[wtm_bf])
        p.op("tensor", lambda t_, t=t: t_.transpose(ps_tb[0:8, 0:128], wtm_bf[:, :], id_bf_sb[:, :]),
             reads=[wtm_bf, id_bf_sb], writes=[ps_tb])
        p.op("vector", lambda v, t=t: v.tensor_copy(w_fm[0:8, t * 128:(t + 1) * 128], ps_tb[0:8, 0:128]),
             reads=[ps_tb], writes=[w_fm])
        if t == 0:
            p.op("vector", lambda v: v.tensor_copy(dbg5_sb[:, :], ps_tb[0:8, 0:128]),
                 reads=[ps_tb], writes=[dbg5_sb])
            p.dma("sync", dbg5_out, dbg5_sb[:, :], reads=[dbg5_sb])

    def ffn_expert(wg_src, wu_src, wd_src, half, is_shared, e=None):
        for k in range(8):
            p.dma("gpsimd", sgb[k][:, :], wg_src[k * 128:(k + 1) * 128, :],
                  reads=[], writes=[sg_reg[k]])
            p.dma("gpsimd", sub[k][:, :], wu_src[k * 128:(k + 1) * 128, :],
                  reads=[], writes=[su_reg[k]])
        for k in range(16):
            p.dma("gpsimd", sdb[k][:, :], wd_src[k * 128:(k + 1) * 128, :],
                  reads=[], writes=[sdb[k]])
        for tci in range(2):
            tc = half * 2 + tci
            t0 = tc * CH
            h0 = tci * CH
            if not is_shared:
                p.op("tensor", lambda t, e=e, t0=t0: t.matmul(
                    ps_w[:, :], sel8_sb[0:8, e * 128:(e + 1) * 128], w_fm[0:8, t0:t0 + CH],
                    start=True, stop=True),
                    reads=[sel8_sb, w_fm], writes=[ps_w])
                p.op("vector", lambda v: v.tensor_copy(w_rep[:, :], ps_w[:, :]),
                     reads=[ps_w], writes=[w_rep])
            for oc in range(16):
                for kd in range(8):
                    p.op("tensor", lambda t, oc=oc, kd=kd, t0=t0: t.matmul(
                        ps_mm[:, :], sgb[kd][:, oc * 128:(oc + 1) * 128], xn0_fm[:, kd, t0:t0 + CH],
                        start=(kd == 0), stop=(kd == 7)),
                        reads=[sg_reg[kd], xn0_fm], writes=[ps_mm])
                for kd in range(8):
                    p.op("tensor", lambda t, oc=oc, kd=kd, t0=t0: t.matmul(
                        ps_u[:, :], sub[kd][:, oc * 128:(oc + 1) * 128], xn0_fm[:, kd, t0:t0 + CH],
                        start=(kd == 0), stop=(kd == 7)),
                        reads=[su_reg[kd], xn0_fm], writes=[ps_u])
                p.op("scalar", lambda s: s.activation(hsil[:, :], ps_mm[:, :], AF.Silu),
                     reads=[ps_mm], writes=[hsil])
                p.op("vector", lambda v, oc=oc: v.tensor_tensor(h_moe[:, oc, :], hsil[:, :],
                                                                ps_u[:, :], op=ALU.mult),
                     reads=[hsil, ps_u], writes=[h_moe])
            for oc in range(8):
                for kc in range(16):
                    p.op("tensor", lambda t, oc=oc, kc=kc: t.matmul(
                        ps_y[:, :], sdb[kc][:, oc * 128:(oc + 1) * 128], h_moe[:, kc, :],
                        start=(kc == 0), stop=(kc == 15)),
                        reads=[sdb[kc], h_moe], writes=[ps_y])
                if is_shared:
                    p.op("vector", lambda v, oc=oc, h0=h0: v.tensor_copy(
                        gacc[:, oc, h0:h0 + CH], ps_y[:, :]),
                        reads=[ps_y], writes=[gacc])
                else:
                    p.op("vector", lambda v: v.tensor_tensor(tmp_y[:, :], ps_y[:, :], w_rep[:, :],
                                                             op=ALU.mult),
                         reads=[ps_y, w_rep], writes=[tmp_y])
                    p.op("vector", lambda v, oc=oc, h0=h0: v.tensor_tensor(
                        gacc[:, oc, h0:h0 + CH], gacc[:, oc, h0:h0 + CH], tmp_y[:, :], op=ALU.add),
                        reads=[gacc, tmp_y], writes=[gacc])

    p.dma("sync", dbg_out[0:8, :], w_fm[0:8, :], reads=[w_fm])
    for half in range(2):
        ffn_expert(sgT, suT, sdT, half, True)
        for e in range(8):
            ffn_expert(wgT[e], wuT[e], wdT[e], half, False, e=e)
        p.dma("sync", gpre_out[:, :, half * (TPC // 2):(half + 1) * (TPC // 2)],
              gacc[:, :, :], reads=[gacc])

    p.finish()
    return p.emit()


def stage_launch1(inp):
    x = np.asarray(inp["L0_attn_x"]).reshape(B * S, D)
    W_in = np.asarray(inp["L0_attn_W_in"])
    shared = {
        "w_inT": np.ascontiguousarray(W_in.T),
        "lnw_rep": np.ascontiguousarray(np.broadcast_to(np.asarray(inp["L0_attn_ln_w"]), (128, D))),
        "lnb_rep": np.ascontiguousarray(np.broadcast_to(np.asarray(inp["L0_attn_ln_b"]), (128, D))),
        "convw": np.ascontiguousarray(
            np.asarray(inp["L0_attn_conv_w"]).reshape(16, 128, 4).transpose(1, 0, 2).reshape(128, 64)),
        "convb": np.ascontiguousarray(np.asarray(inp["L0_attn_conv_b"]).reshape(16, 128).T),
        "wdtT": np.ascontiguousarray(np.asarray(inp["L0_attn_W_dt"]).T),
        "bdt": np.ascontiguousarray(np.asarray(inp["L0_attn_b_dt"]).reshape(16, 128).T),
        "sgT": np.ascontiguousarray(np.asarray(inp["L0_ffn_sg_w"]).T),
        "suT": np.ascontiguousarray(np.asarray(inp["L0_ffn_su_w"]).T),
        "sdT": np.ascontiguousarray(np.asarray(inp["L0_ffn_sd_w"]).T),
        "rtT": np.ascontiguousarray(np.asarray(inp["L0_ffn_router_w"]).T),
        "rbias": np.ascontiguousarray(np.asarray(inp["L0_ffn_routing_bias"]).reshape(8, 1)),
        "wgT": np.ascontiguousarray(np.asarray(inp["L0_ffn_wg"]).transpose(0, 2, 1)),
        "wuT": np.ascontiguousarray(np.asarray(inp["L0_ffn_wu"]).transpose(0, 2, 1)),
        "wdT": np.ascontiguousarray(np.asarray(inp["L0_ffn_wd"]).transpose(0, 2, 1)),
        "id_bf": np.eye(128, dtype=BF),
        "id_f32": np.eye(128, dtype=np.float32),
    }
    wx = np.asarray(inp["L0_attn_W_xproj"])
    wxT = np.zeros((DI, 128), np.float32)
    wxT[:, 0:64] = wx[0:64].T
    wxT[:, 64:80] = wx[64:80].T
    wxT[:, 96:112] = wx[80:96].T
    shared["wxT"] = wxT
    sel8 = np.zeros((8, 8 * 128), BF)
    for e in range(8):
        sel8[e, e * 128:(e + 1) * 128] = 1
    shared["sel8"] = sel8

    in_maps = []
    for c in range(NCORES):
        r0 = c * TPC
        xr = np.zeros((TPC + 3, D), np.float32)
        xr[3:] = x[r0:r0 + TPC]
        if c % 2 == 1:
            xr[0:3] = x[r0 - 3:r0]
        m = dict(shared)
        m["x"] = xr
        m["hmask"] = np.full((3, 1), 1.0 if c % 2 == 1 else 0.0, np.float32)
        in_maps.append(m)
    return in_maps




S2CH = 512
S2N = S // S2CH


def build_launch2():
    p = Prog()
    nc = p.nc

    dt_in = nc.dram_tensor("dt", [128, 8, S], BF16, kind="ExternalInput").ap()
    xc_in = nc.dram_tensor("xconv", [128, 8, S], BF16, kind="ExternalInput").ap()
    zs_in = nc.dram_tensor("zsilu", [128, 8, S], BF16, kind="ExternalInput").ap()
    bmcm_in = nc.dram_tensor("bmcm", [64, S], F32, kind="ExternalInput").ap()
    alog_in = nc.dram_tensor("alog", [128, 8 * 16], F32, kind="ExternalInput").ap()
    dsk_in = nc.dram_tensor("dsk", [128, 8], F32, kind="ExternalInput").ap()
    sel64 = nc.dram_tensor("sel64", [64, 32 * 128], BF16, kind="ExternalInput").ap()
    yg_out = nc.dram_tensor("yg", [128, 8, S], BF16, kind="ExternalOutput").ap()

    dt_sb = p.sb("dt_sb", [128, 8, S2CH], BF16)
    xc_sb = p.sb("xc_sb", [128, 8, S2CH], BF16)
    zs_sb = p.sb("zs_sb", [128, 8, S2CH], BF16)
    bmcm_sb = p.sb("bmcm_sb", [64, S2CH], F32)
    bmcm_bf = p.sb("bmcm_bf", [64, S2CH], BF16)
    alog_sb = p.sb("alog_sb", [128, 8, 16], F32)
    acol = p.sb("acol", [128, 8, 16], F32)
    dsk_sb = p.sb("dsk_sb", [128, 8], F32)
    sel64_sb = p.sb("sel64_sb", [64, 32 * 128], BF16)
    u_sb = p.sb("u_sb", [128, 8, S2CH], BF16)
    da_sb = p.sb("da_sb", [128, S2CH], F32)
    db_sb = p.sb("db_sb", [128, S2CH], F32)
    h_ch = p.sb("h_ch", [128, 16, S2CH], F32)
    yacc = p.sb("yacc", [128, S2CH], F32)
    tmp_sb = p.sb("tmp_sb", [128, S2CH], F32)
    state = p.sb("state", [128, 128], F32)
    yg_sb = p.sb("yg_sb", [128, 8, S2CH], BF16)
    ps_bm = p.ps("ps_bm")
    ps_cm = p.ps("ps_cm")

    p.dma("gpsimd", sel64_sb[:, :], sel64, writes=[sel64_sb])
    p.dma("sync", alog_sb[:, :, :].rearrange("p a b -> p (a b)"), alog_in, writes=[alog_sb])
    p.dma("sync", dsk_sb[:, :], dsk_in, writes=[dsk_sb])
    p.op("scalar", lambda s: s.activation(acol[:, :, :].rearrange("p a b -> p (a b)"),
                                          alog_sb[:, :, :].rearrange("p a b -> p (a b)"), AF.Exp),
         reads=[alog_sb], writes=[acol])
    p.op("vector", lambda v: v.tensor_single_scalar(acol[:, :, :].rearrange("p a b -> p (a b)"),
                                                    acol[:, :, :].rearrange("p a b -> p (a b)"),
                                                    -1.0, op=ALU.mult),
         reads=[acol], writes=[acol])
    p.op("vector", lambda v: v.memset(state[:, :], 0.0), writes=[state])

    for ck in range(S2N):
        t0 = ck * S2CH
        p.dma("sync", dt_sb[:, :, :], dt_in[:, :, t0:t0 + S2CH], writes=[dt_sb])
        p.dma("sync", xc_sb[:, :, :], xc_in[:, :, t0:t0 + S2CH], writes=[xc_sb])
        p.dma("gpsimd", zs_sb[:, :, :], zs_in[:, :, t0:t0 + S2CH], writes=[zs_sb])
        p.dma("sync", bmcm_sb[:, :], bmcm_in[:, t0:t0 + S2CH], writes=[bmcm_sb])
        p.op("vector", lambda v: v.tensor_copy(bmcm_bf[:, :], bmcm_sb[:, :]),
             reads=[bmcm_sb], writes=[bmcm_bf])
        for ch in range(8):
            p.op("vector", lambda v, ch=ch: v.tensor_tensor(u_sb[:, ch, :], dt_sb[:, ch, :],
                                                            xc_sb[:, ch, :], op=ALU.mult),
                 reads=[dt_sb, xc_sb], writes=[u_sb])
            for n in range(16):
                p.op("tensor", lambda t, n=n: t.matmul(
                    ps_bm[:, :], sel64_sb[:, n * 128:(n + 1) * 128], bmcm_bf[:, :],
                    start=True, stop=True),
                    reads=[sel64_sb, bmcm_bf], writes=[ps_bm])
                p.op("tensor", lambda t, n=n: t.matmul(
                    ps_cm[:, :], sel64_sb[:, (16 + n) * 128:(17 + n) * 128], bmcm_bf[:, :],
                    start=True, stop=True),
                    reads=[sel64_sb, bmcm_bf], writes=[ps_cm])
                p.op("scalar", lambda s, ch=ch, n=n: s.activation(
                    da_sb[:, :], dt_sb[:, ch, :], AF.Exp, scale=acol[:, ch, n:n + 1]),
                    reads=[dt_sb, acol], writes=[da_sb])
                p.op("vector", lambda v, ch=ch: v.tensor_tensor(db_sb[:, :], u_sb[:, ch, :],
                                                                ps_bm[:, :], op=ALU.mult),
                     reads=[u_sb, ps_bm], writes=[db_sb])
                p.op("vector", lambda v, ch=ch, n=n: v.tensor_tensor_scan(
                    h_ch[:, n, :], da_sb[:, :], db_sb[:, :], state[:, ch * 16 + n:ch * 16 + n + 1],
                    op0=ALU.mult, op1=ALU.add),
                    reads=[da_sb, db_sb, state], writes=[h_ch])
                p.op("vector", lambda v, n=n: v.tensor_tensor(tmp_sb[:, :], h_ch[:, n, :],
                                                              ps_cm[:, :], op=ALU.mult),
                     reads=[h_ch, ps_cm], writes=[tmp_sb])
                if n == 0:
                    p.op("vector", lambda v: v.tensor_copy(yacc[:, :], tmp_sb[:, :]),
                         reads=[tmp_sb], writes=[yacc])
                else:
                    p.op("vector", lambda v: v.tensor_tensor(yacc[:, :], yacc[:, :], tmp_sb[:, :],
                                                             op=ALU.add),
                         reads=[yacc, tmp_sb], writes=[yacc])
            p.op("vector", lambda v, ch=ch: v.tensor_copy(
                state[:, ch * 16:(ch + 1) * 16], h_ch[:, :, S2CH - 1]),
                reads=[h_ch], writes=[state])
            p.op("vector", lambda v, ch=ch: v.scalar_tensor_tensor(
                yacc[:, :], xc_sb[:, ch, :], dsk_sb[:, ch:ch + 1], yacc[:, :],
                op0=ALU.mult, op1=ALU.add),
                reads=[xc_sb, dsk_sb, yacc], writes=[yacc])
            p.op("vector", lambda v, ch=ch: v.tensor_tensor(yg_sb[:, ch, :], yacc[:, :],
                                                            zs_sb[:, ch, :], op=ALU.mult),
                 reads=[yacc, zs_sb], writes=[yg_sb])
        p.dma("sync", yg_out[:, :, t0:t0 + S2CH], yg_sb[:, :, :], reads=[yg_sb])

    p.finish()
    return p.emit()


def stage_launch2(inp, res1):
    alog = np.asarray(inp["L0_attn_A_log"]).reshape(16, 128, 16)
    dsk = np.asarray(inp["L0_attn_Dskip"]).reshape(16, 128)
    sel64 = np.zeros((64, 32 * 128), BF)
    for n in range(16):
        sel64[n, n * 128:(n + 1) * 128] = 1
        sel64[32 + n, (16 + n) * 128:(17 + n) * 128] = 1
    in_maps = []
    for c in range(NCORES):
        b, hc = c // 2, c % 2
        oA, oB = res1[2 * b], res1[2 * b + 1]
        m = {
            "dt": np.concatenate([oA["dt"][:, 8 * hc:8 * hc + 8, :], oB["dt"][:, 8 * hc:8 * hc + 8, :]], axis=2),
            "xconv": np.concatenate([oA["xconv"][:, 8 * hc:8 * hc + 8, :], oB["xconv"][:, 8 * hc:8 * hc + 8, :]], axis=2),
            "zsilu": np.concatenate([oA["zsilu"][:, 8 * hc:8 * hc + 8, :], oB["zsilu"][:, 8 * hc:8 * hc + 8, :]], axis=2),
            "bmcm": np.concatenate([oA["bmcm"], oB["bmcm"]], axis=1),
            "alog": np.ascontiguousarray(alog[8 * hc:8 * hc + 8].transpose(1, 0, 2).reshape(128, 128)),
            "dsk": np.ascontiguousarray(dsk[8 * hc:8 * hc + 8].T),
            "sel64": sel64,
        }
        in_maps.append(m)
    return in_maps




def build_launch3():
    p = Prog()
    nc = p.nc

    x_in = nc.dram_tensor("x", [TPC, D], F32, kind="ExternalInput").ap()
    yg_in = nc.dram_tensor("yg", [128, 16, TPC], BF16, kind="ExternalInput").ap()
    gpre_in = nc.dram_tensor("gpre", [128, 8, TPC], F32, kind="ExternalInput").ap()
    woT = nc.dram_tensor("woT", [DI, D], F32, kind="ExternalInput").ap()
    bwT = nc.dram_tensor("bwT", [D, D], F32, kind="ExternalInput").ap()
    fc1q = nc.dram_tensor("fc1q", [D, 4096], I32, kind="ExternalInput").ap()
    fc2q = nc.dram_tensor("fc2q", [4096, D], I32, kind="ExternalInput").ap()
    fc1s = nc.dram_tensor("fc1s", [1, 8 * 4096], F32, kind="ExternalInput").ap()
    fc2s = nc.dram_tensor("fc2s", [1, 32 * 1024], F32, kind="ExternalInput").ap()
    fc1b = nc.dram_tensor("fc1b", [128, 32], F32, kind="ExternalInput").ap()
    fc2b = nc.dram_tensor("fc2b", [128, 8], F32, kind="ExternalInput").ap()
    gam_rep = nc.dram_tensor("gam_rep", [128, D], F32, kind="ExternalInput").ap()
    bet_rep = nc.dram_tensor("bet_rep", [128, D], F32, kind="ExternalInput").ap()
    id_bf = nc.dram_tensor("id_bf", [128, 128], BF16, kind="ExternalInput").ap()
    id_f32 = nc.dram_tensor("id_f32", [128, 128], F32, kind="ExternalInput").ap()
    ones1 = nc.dram_tensor("ones1", [1, 128], BF16, kind="ExternalInput").ap()

    x1_out = nc.dram_tensor("x1", [TPC, D], F32, kind="ExternalOutput").ap()
    bx_out = nc.dram_tensor("bx", [128, 8, TPC], F32, kind="ExternalOutput").ap()
    g1_out = nc.dram_tensor("g1", [128, 8, TPC], BF16, kind="ExternalOutput").ap()

    w1_scr = nc.dram_tensor("w1_scr", [D, 4096], BF16, kind="Internal").ap()
    w2_scr = nc.dram_tensor("w2_scr", [4096, D], BF16, kind="Internal").ap()

    id_bf_sb = p.sb("id_bf_sb", [128, 128], BF16)
    id_f32_sb = p.sb("id_f32_sb", [128, 128], F32)
    ones1_sb = p.sb("ones1_sb", [1, 128], BF16)
    scr_sb = p.sb("scr_sb", [1, 512], BF16)
    fc1b_sb = p.sb("fc1b_sb", [128, 32], F32)
    fc2b_sb = p.sb("fc2b_sb", [128, 8], F32)
    gam_sb = p.sb("gam_sb", [128, D], BF16)
    bet_sb = p.sb("bet_sb", [128, D], BF16)
    stats = p.sb("stats3", [128, 16], F32)
    sq_tmp = p.sb("sq_tmp3", [128, CH], F32)

    p.dma("sync", id_bf_sb[:, :], id_bf, writes=[id_bf_sb])
    p.dma("sync", id_f32_sb[:, :], id_f32, writes=[id_f32_sb])
    p.dma("sync", ones1_sb[:, :], ones1, writes=[ones1_sb])
    p.dma("sync", fc1b_sb[:, :], fc1b, writes=[fc1b_sb])
    p.dma("sync", fc2b_sb[:, :], fc2b, writes=[fc2b_sb])
    p.dma("gpsimd", gam_sb[:, :], gam_rep, writes=[gam_sb])
    p.dma("gpsimd", bet_sb[:, :], bet_rep, writes=[bet_sb])

    one_col3 = p.const_col(1.0)
    eps_col3 = p.const_col(EPS)

    ps_a = p.ps("ps3_a")
    ps_b = p.ps("ps3_b")
    ps_sc = p.ps("ps3_sc")
    ps_tb3 = nc.alloc_psum_tensor("ps3_tb", [128, 1024], BF16)
    ps_tf3 = p.ps("ps3_tf")
    ps_tf4 = p.ps("ps3_tf4")

    with ExitStack() as dstk:
        qt = dstk.enter_context(nc.sbuf_tensor("qt", [128, 512], I32))
        qf = dstk.enter_context(nc.sbuf_tensor("qf", [128, 512], F32))
        wt = dstk.enter_context(nc.sbuf_tensor("wt", [128, 512], BF16))
        for g in range(8):
            for c in range(8):
                p.dma("sync", qt[:, :], fc1q[g * 128:(g + 1) * 128, c * 512:(c + 1) * 512], writes=[qt])
                p.dma("gpsimd", scr_sb[:, :], fc1s[0:1, g * 4096 + c * 512:g * 4096 + (c + 1) * 512],
                      writes=[scr_sb])
                p.op("tensor", lambda t: t.matmul(
                    ps_sc[:, :], ones1_sb[0:1, :], scr_sb[0:1, :],
                    start=True, stop=True),
                    reads=[ones1_sb, scr_sb], writes=[ps_sc])
                p.op("vector", lambda v: v.tensor_scalar(qf[:, :], qt[:, :], 128.0, None, op0=ALU.subtract),
                     reads=[qt], writes=[qf])
                p.op("vector", lambda v: v.tensor_tensor(wt[:, :], qf[:, :], ps_sc[:, :], op=ALU.mult),
                     reads=[qf, ps_sc], writes=[wt])
                p.dma("sync", w1_scr[g * 128:(g + 1) * 128, c * 512:(c + 1) * 512], wt[:, :], reads=[wt])
        for g in range(32):
            for c in range(2):
                p.dma("sync", qt[:, :], fc2q[g * 128:(g + 1) * 128, c * 512:(c + 1) * 512], writes=[qt])
                p.dma("gpsimd", scr_sb[:, :], fc2s[0:1, g * 1024 + c * 512:g * 1024 + (c + 1) * 512],
                      writes=[scr_sb])
                p.op("tensor", lambda t: t.matmul(
                    ps_sc[:, :], ones1_sb[0:1, :], scr_sb[0:1, :],
                    start=True, stop=True),
                    reads=[ones1_sb, scr_sb], writes=[ps_sc])
                p.op("vector", lambda v: v.tensor_scalar(qf[:, :], qt[:, :], 128.0, None, op0=ALU.subtract),
                     reads=[qt], writes=[qf])
                p.op("vector", lambda v: v.tensor_tensor(wt[:, :], qf[:, :], ps_sc[:, :], op=ALU.mult),
                     reads=[qf, ps_sc], writes=[wt])
                p.dma("sync", w2_scr[g * 128:(g + 1) * 128, c * 512:(c + 1) * 512], wt[:, :], reads=[wt])
    p.barrier()

    with ExitStack() as stk:
        def tmp(name, shape, dtype):
            return stk.enter_context(nc.sbuf_tensor(name, shape, dtype))
        x_t = tmp("x3_t", [128, D], F32)
        xn0_t = tmp("xn0_3t", [128, D], F32)
        xn0bf_t = tmp("xn0bf3", [128, D], BF16)
        xn0_fm = tmp("xn0fm3", [128, 8, CH], BF16)
        yg_sb = tmp("yg3", [128, 16, CH], BF16)
        gpre_sb = tmp("gpre3", [128, 8, CH], F32)
        s_fm = tmp("sfm3", [128, 8, CH], F32)
        x1_t = tmp("x1_3t", [128, D], F32)
        xn1_t = tmp("xn1_3t", [128, D], F32)
        xn1bf_t = tmp("xn1bf3", [128, D], BF16)
        xn1_fm = tmp("xn1fm3", [128, 8, CH], BF16)
        hln_t = tmp("hln3", [128, D], F32)
        hlnbf_t = tmp("hlnbf3", [128, D], BF16)
        hln_fm = tmp("hlnfm3", [128, 8, CH], BF16)
        y1_fm = tmp("y1fm3", [128, 32, CH], BF16)
        g1_sb = tmp("g13", [128, 8, CH], BF16)
        bx_sb = tmp("bx3", [128, 8, CH], F32)
        wo_sb = [tmp(f"wo3_{i}", [128, 16, 128], BF16) for i in range(2)]
        bw_sb = [tmp(f"bw3_{i}", [128, 8, 128], BF16) for i in range(2)]
        w1_sb = [tmp(f"w13_{i}", [128, 8, 128], BF16) for i in range(2)]
        w2_sb = [tmp(f"w23_{i}", [128, 32, 128], BF16) for i in range(2)]

        def transpose_tm_to_fm(src_bf, src_reg, dst_fm, dst_reg, col0):
            for kd in range(8):
                p.op("tensor", lambda t_, kd=kd, src_bf=src_bf: t_.transpose(
                    ps_tb3[:, kd * 128:(kd + 1) * 128], src_bf[:, kd * 128:(kd + 1) * 128],
                    id_bf_sb[:, :]),
                    reads=[src_reg, id_bf_sb], writes=[ps_tb3])
            p.op("vector", lambda v, col0=col0, dst_fm=dst_fm: v.tensor_copy(
                dst_fm[:, :, col0:col0 + 128],
                ps_tb3[:, :].rearrange("p (a b) -> p a b", a=8)),
                reads=[ps_tb3], writes=[dst_reg])

        for ck in range(NCHUNK):
            tok0 = ck * CH
            p.dma("sync", yg_sb[:, :, :], yg_in[:, :, tok0:tok0 + CH], writes=[yg_sb])
            p.dma("gpsimd", gpre_sb[:, :, :], gpre_in[:, :, tok0:tok0 + CH], writes=[gpre_sb])
            for tt in range(4):
                t = ck * 4 + tt
                p.dma("sync", x_t[:, :], x_in[t * 128:(t + 1) * 128, :], writes=[x_t])
                m, r = p.ln_cols(x_t[:, :], x_t, D, stats, 0, sq_tmp, eps_col3)
                p.op("vector", lambda v, m=m, r=r: v.tensor_scalar(xn0_t[:, :], x_t[:, :], m, r,
                                                                   op0=ALU.subtract, op1=ALU.mult),
                     reads=[x_t, stats], writes=[xn0_t])
                p.op("vector", lambda v: v.tensor_copy(xn0bf_t[:, :], xn0_t[:, :]),
                     reads=[xn0_t], writes=[xn0bf_t])
                transpose_tm_to_fm(xn0bf_t, xn0bf_t, xn0_fm, xn0_fm, tt * 128)

            for n in range(8):
                buf = wo_sb[n % 2]
                breg = ("wo3", n % 2)
                p.dma("gpsimd", buf[:, :, :], woT.rearrange("(c p) o -> p c o", p=128)[:, :, n * 128:(n + 1) * 128],
                      writes=[breg])
                for k in range(16):
                    p.op("tensor", lambda t, n=n, k=k, buf=buf, breg=breg: t.matmul(
                        ps_a[:, :], buf[:, k, :], yg_sb[:, k, :],
                        start=(k == 0), stop=(k == 15)),
                        reads=[breg, yg_sb], writes=[ps_a])
                p.op("vector", lambda v, n=n: v.tensor_tensor(s_fm[:, n, :], ps_a[:, :],
                                                              xn0_fm[:, n, :], op=ALU.add),
                     reads=[ps_a, xn0_fm], writes=[s_fm])
            p.op("scalar", lambda s: s.activation(gpre_sb[:, :, :].rearrange("p a b -> p (a b)"),
                                                  gpre_sb[:, :, :].rearrange("p a b -> p (a b)"), AF.Silu),
                 reads=[gpre_sb], writes=[gpre_sb])
            p.op("vector", lambda v: v.tensor_tensor(s_fm[:, :, :].rearrange("p a b -> p (a b)"),
                                                     s_fm[:, :, :].rearrange("p a b -> p (a b)"),
                                                     gpre_sb[:, :, :].rearrange("p a b -> p (a b)"), op=ALU.mult),
                 reads=[s_fm, gpre_sb], writes=[s_fm])

            for tt in range(4):
                t = ck * 4 + tt
                p.dma("scalar", x_t[:, :], x_in[t * 128:(t + 1) * 128, :], writes=[x_t])
                for oc in range(8):
                    bank = ps_tf3 if oc < 4 else ps_tf4
                    p.op("tensor", lambda t_, oc=oc, tt=tt, bank=bank: t_.transpose(
                        bank[:, (oc % 4) * 128:(oc % 4 + 1) * 128],
                        s_fm[:, oc, tt * 128:(tt + 1) * 128], id_f32_sb[:, :]),
                        reads=[s_fm, id_f32_sb], writes=[bank])
                p.op("vector", lambda v: v.tensor_tensor(x1_t[:, 0:512], x_t[:, 0:512],
                                                         ps_tf3[:, :], op=ALU.add),
                     reads=[x_t, ps_tf3], writes=[x1_t])
                p.op("vector", lambda v: v.tensor_tensor(x1_t[:, 512:1024], x_t[:, 512:1024],
                                                         ps_tf4[:, :], op=ALU.add),
                     reads=[x_t, ps_tf4], writes=[x1_t])
                p.dma("sync", x1_out[t * 128:(t + 1) * 128, :], x1_t[:, :], reads=[x1_t])
                m, r = p.ln_cols(x1_t[:, :], x1_t, D, stats, 4, sq_tmp, eps_col3)
                p.op("vector", lambda v, m=m, r=r: v.tensor_scalar(xn1_t[:, :], x1_t[:, :], m, r,
                                                                   op0=ALU.subtract, op1=ALU.mult),
                     reads=[x1_t, stats], writes=[xn1_t])
                p.op("vector", lambda v: v.tensor_copy(xn1bf_t[:, :], xn1_t[:, :]),
                     reads=[xn1_t], writes=[xn1bf_t])
                transpose_tm_to_fm(xn1bf_t, xn1bf_t, xn1_fm, xn1_fm, tt * 128)
                m2, r2 = p.ln_cols(xn1_t[:, :], xn1_t, D, stats, 8, sq_tmp, eps_col3)
                p.op("vector", lambda v, m2=m2, r2=r2: v.tensor_scalar(hln_t[:, :], xn1_t[:, :], m2, r2,
                                                                      op0=ALU.subtract, op1=ALU.mult),
                     reads=[xn1_t, stats], writes=[hln_t])
                p.op("vector", lambda v: v.tensor_tensor(hln_t[:, :], hln_t[:, :], gam_sb[:, :], op=ALU.mult),
                     reads=[hln_t, gam_sb], writes=[hln_t])
                p.op("vector", lambda v: v.tensor_tensor(hln_t[:, :], hln_t[:, :], bet_sb[:, :], op=ALU.add),
                     reads=[hln_t, bet_sb], writes=[hln_t])
                p.op("vector", lambda v: v.tensor_copy(hlnbf_t[:, :], hln_t[:, :]),
                     reads=[hln_t], writes=[hlnbf_t])
                transpose_tm_to_fm(hlnbf_t, hlnbf_t, hln_fm, hln_fm, tt * 128)

            for n in range(8):
                buf = bw_sb[n % 2]
                breg = ("bw3", n % 2)
                p.dma("gpsimd", buf[:, :, :], bwT.rearrange("(c p) o -> p c o", p=128)[:, :, n * 128:(n + 1) * 128],
                      writes=[breg])
                for k in range(8):
                    p.op("tensor", lambda t, n=n, k=k, buf=buf, breg=breg: t.matmul(
                        ps_b[:, :], buf[:, k, :], xn1_fm[:, k, :],
                        start=(k == 0), stop=(k == 7)),
                        reads=[breg, xn1_fm], writes=[ps_b])
                p.op("vector", lambda v, n=n: v.tensor_copy(bx_sb[:, n, :], ps_b[:, :]),
                     reads=[ps_b], writes=[bx_sb])
            p.dma("sync", bx_out[:, :, tok0:tok0 + CH], bx_sb[:, :, :], reads=[bx_sb])

            for n in range(32):
                buf = w1_sb[n % 2]
                breg = ("w13", n % 2)
                p.dma("gpsimd", buf[:, :, :], w1_scr.rearrange("(c p) o -> p c o", p=128)[:, :, n * 128:(n + 1) * 128],
                      writes=[breg])
                for k in range(8):
                    p.op("tensor", lambda t, n=n, k=k, buf=buf, breg=breg: t.matmul(
                        ps_a[:, :], buf[:, k, :], hln_fm[:, k, :],
                        start=(k == 0), stop=(k == 7)),
                        reads=[breg, hln_fm], writes=[ps_a])
                p.op("scalar", lambda s, n=n: s.activation(y1_fm[:, n, :], ps_a[:, :], AF.Silu,
                                                           bias=fc1b_sb[:, n:n + 1]),
                     reads=[ps_a, fc1b_sb], writes=[y1_fm])
            for n in range(8):
                buf = w2_sb[n % 2]
                breg = ("w23", n % 2)
                p.dma("gpsimd", buf[:, :, :], w2_scr.rearrange("(c p) o -> p c o", p=128)[:, :, n * 128:(n + 1) * 128],
                      writes=[breg])
                for k in range(32):
                    p.op("tensor", lambda t, n=n, k=k, buf=buf, breg=breg: t.matmul(
                        ps_b[:, :], buf[:, k, :], y1_fm[:, k, :],
                        start=(k == 0), stop=(k == 31)),
                        reads=[breg, y1_fm], writes=[ps_b])
                p.op("vector", lambda v, n=n: v.tensor_scalar(bx_sb[:, 0, :] if False else s_fm[:, n, :],
                                                              ps_b[:, :], fc2b_sb[:, n:n + 1], 0.5,
                                                              op0=ALU.add, op1=ALU.mult),
                     reads=[ps_b, fc2b_sb], writes=[s_fm])
                p.op("vector", lambda v, n=n: v.tensor_tensor(s_fm[:, n, :], s_fm[:, n, :],
                                                              xn1_fm[:, n, :], op=ALU.add),
                     reads=[s_fm, xn1_fm], writes=[s_fm])
                p.op("scalar", lambda s, n=n: s.activation(g1_sb[:, n, :], s_fm[:, n, :], AF.Silu),
                     reads=[s_fm], writes=[g1_sb])
            p.dma("sync", g1_out[:, :, tok0:tok0 + CH], g1_sb[:, :, :], reads=[g1_sb])

    p.finish()
    return p.emit()


def stage_launch3(inp, res1, res2):
    x = np.asarray(inp["L0_attn_x"]).reshape(B * S, D)
    shared = {
        "woT": np.ascontiguousarray(np.asarray(inp["L0_attn_W_out"]).T),
        "bwT": np.ascontiguousarray(np.asarray(inp["L1_attn_B_w"]).T),
        "fc1q": np.ascontiguousarray(np.asarray(inp["L1_ffn_fc1_q"], np.int32)),
        "fc2q": np.ascontiguousarray(np.asarray(inp["L1_ffn_fc2_q"], np.int32)),
        "fc1s": np.ascontiguousarray(np.asarray(inp["L1_ffn_fc1_scales"]).reshape(1, -1)),
        "fc2s": np.ascontiguousarray(np.asarray(inp["L1_ffn_fc2_scales"]).reshape(1, -1)),
        "fc1b": np.ascontiguousarray(np.asarray(inp["L1_ffn_fc1_bias"]).reshape(32, 128).T),
        "fc2b": np.ascontiguousarray(np.asarray(inp["L1_ffn_fc2_bias"]).reshape(8, 128).T),
        "gam_rep": np.ascontiguousarray(np.broadcast_to(np.asarray(inp["L1_ffn_ln_gamma"]), (128, D))),
        "bet_rep": np.ascontiguousarray(np.broadcast_to(np.asarray(inp["L1_ffn_ln_beta"]), (128, D))),
        "id_bf": np.eye(128, dtype=BF),
        "id_f32": np.eye(128, dtype=np.float32),
        "ones1": np.ones((1, 128), BF),
    }
    in_maps = []
    for c in range(NCORES):
        b, half = c // 2, c % 2
        t0 = half * 2048
        ygA = res2[2 * b]["yg"][:, :, t0:t0 + 2048]
        ygB = res2[2 * b + 1]["yg"][:, :, t0:t0 + 2048]
        m = dict(shared)
        m["x"] = np.ascontiguousarray(x[c * TPC:(c + 1) * TPC])
        m["yg"] = np.ascontiguousarray(np.concatenate([ygA, ygB], axis=1))
        m["gpre"] = res1[c]["gpre"]
        in_maps.append(m)
    return in_maps




L4SEG = 512
L4N = S // L4SEG


def build_launch4():
    p = Prog()
    nc = p.nc

    bx_in = nc.dram_tensor("bx", [128, 8, S], F32, kind="ExternalInput").ap()
    a_in = nc.dram_tensor("a1", [128, 8], F32, kind="ExternalInput").ap()
    woT = nc.dram_tensor("woT1", [D, D], F32, kind="ExternalInput").ap()
    bo_in = nc.dram_tensor("bo", [128, 8], F32, kind="ExternalInput").ap()
    x1_in = nc.dram_tensor("x1", [S, D], F32, kind="ExternalInput").ap()
    g1_in = nc.dram_tensor("g1", [128, 8, S], BF16, kind="ExternalInput").ap()
    id_f32 = nc.dram_tensor("id_f32", [128, 128], F32, kind="ExternalInput").ap()
    x2_out = nc.dram_tensor("x2", [S, D], F32, kind="ExternalOutput").ap()

    a_sb = p.sb("a_sb", [128, 8], F32)
    dec = p.sb("dec", [128, 8], F32)
    wo_sb = p.sb("wo4_sb", [128, 8, D], BF16)
    bo_sb = p.sb("bo_sb", [128, 8], F32)
    id_sb = p.sb("id4_sb", [128, 128], F32)
    bx_sb = p.sb("bx4_sb", [128, 8, L4SEG], F32)
    h_sb = p.sb("h4_sb", [128, 8, L4SEG], F32)
    h_bf = p.sb("h4_bf", [128, 8, L4SEG], BF16)
    g_st = p.sb("g_st", [128, 8], F32)
    pre = p.sb("pre4", [128, 8], F32)
    sg_fm = p.sb("sg4_fm", [128, 8, L4SEG], F32)
    g1_sb = p.sb("g14_sb", [128, 8, L4SEG], BF16)
    x1_t = p.sb("x14_t", [128, D], F32)
    x2_t = p.sb("x24_t", [128, D], F32)
    ps_wo = p.ps("ps4_wo")
    ps_t1 = p.ps("ps4_t1")
    ps_t2 = p.ps("ps4_t2")

    p.dma("sync", a_sb[:, :], a_in, writes=[a_sb])
    p.dma("gpsimd", wo_sb[:, :, :], woT.rearrange("(c p) o -> p c o", p=128), writes=[wo_sb])
    p.dma("sync", bo_sb[:, :], bo_in, writes=[bo_sb])
    p.dma("sync", id_sb[:, :], id_f32, writes=[id_sb])
    p.op("vector", lambda v: v.tensor_scalar(dec[:, :], a_sb[:, :], 10.0, None, op0=ALU.min),
         reads=[a_sb], writes=[dec])
    p.op("scalar", lambda s: s.activation(dec[:, :], dec[:, :], AF.Exp, scale=-1.0),
         reads=[dec], writes=[dec])
    p.op("vector", lambda v: v.memset(g_st[:, :], 0.0), writes=[g_st])

    for sg in range(L4N):
        t0 = sg * L4SEG
        p.dma("sync", bx_sb[:, :, :], bx_in[:, :, t0:t0 + L4SEG], writes=[bx_sb])
        p.dma("gpsimd", g1_sb[:, :, :], g1_in[:, :, t0:t0 + L4SEG], writes=[g1_sb])
        for t in range(L4SEG):
            p.op("vector", lambda v, t=t: v.tensor_tensor(pre[:, :], g_st[:, :], bx_sb[:, :, t],
                                                          op=ALU.add),
                 reads=[g_st, bx_sb], writes=[pre])
            p.op("scalar", lambda s, t=t: s.activation(h_sb[:, :, t], pre[:, :], AF.Tanh),
                 reads=[pre], writes=[("h", t)])
            p.op("vector", lambda v, t=t: v.tensor_tensor(g_st[:, :], h_sb[:, :, t], dec[:, :],
                                                          op=ALU.mult),
                 reads=[("h", t), dec], writes=[g_st])
        h_regs = [("h", t) for t in range(L4SEG)]
        p.op("vector", lambda v: v.tensor_copy(h_bf[:, :, :].rearrange("p a b -> p (a b)"),
                                               h_sb[:, :, :].rearrange("p a b -> p (a b)")),
             reads=h_regs, writes=[h_bf])
        for oc in range(8):
            for k in range(8):
                p.op("tensor", lambda t, oc=oc, k=k: t.matmul(
                    ps_wo[:, :], wo_sb[:, k, oc * 128:(oc + 1) * 128], h_bf[:, k, :],
                    start=(k == 0), stop=(k == 7)),
                    reads=[wo_sb, h_bf], writes=[ps_wo])
            p.op("vector", lambda v, oc=oc: v.tensor_scalar(sg_fm[:, oc, :], ps_wo[:, :],
                                                            bo_sb[:, oc:oc + 1], None, op0=ALU.add),
                 reads=[ps_wo, bo_sb], writes=[sg_fm])
            p.op("vector", lambda v, oc=oc: v.tensor_tensor(sg_fm[:, oc, :], sg_fm[:, oc, :],
                                                            g1_sb[:, oc, :], op=ALU.mult),
                 reads=[sg_fm, g1_sb], writes=[sg_fm])
        for tt in range(4):
            t = sg * 4 + tt
            p.dma("scalar", x1_t[:, :], x1_in[t * 128:(t + 1) * 128, :], writes=[x1_t])
            for oc in range(8):
                bank = ps_t1 if oc < 4 else ps_t2
                p.op("tensor", lambda t_, oc=oc, tt=tt, bank=bank: t_.transpose(
                    bank[:, (oc % 4) * 128:(oc % 4 + 1) * 128],
                    sg_fm[:, oc, tt * 128:(tt + 1) * 128], id_sb[:, :]),
                    reads=[sg_fm, id_sb], writes=[bank])
            p.op("vector", lambda v: v.tensor_tensor(x2_t[:, 0:512], x1_t[:, 0:512],
                                                     ps_t1[:, :], op=ALU.add),
                 reads=[x1_t, ps_t1], writes=[x2_t])
            p.op("vector", lambda v: v.tensor_tensor(x2_t[:, 512:1024], x1_t[:, 512:1024],
                                                     ps_t2[:, :], op=ALU.add),
                 reads=[x1_t, ps_t2], writes=[x2_t])
            p.dma("sync", x2_out[t * 128:(t + 1) * 128, :], x2_t[:, :], reads=[x2_t])

    p.finish()
    return p.emit()


def stage_launch4(inp, res3):
    shared = {
        "a1": np.ascontiguousarray(np.asarray(inp["L1_attn_A"]).reshape(8, 128).T),
        "woT1": np.ascontiguousarray(np.asarray(inp["L1_attn_Wo"]).T),
        "bo": np.ascontiguousarray(np.asarray(inp["L1_attn_bo"]).reshape(8, 128).T),
        "id_f32": np.eye(128, dtype=np.float32),
    }
    in_maps = []
    for b in range(4):
        m = dict(shared)
        m["bx"] = np.ascontiguousarray(np.concatenate(
            [res3[2 * b]["bx"], res3[2 * b + 1]["bx"]], axis=2))
        m["g1"] = np.ascontiguousarray(np.concatenate(
            [res3[2 * b]["g1"], res3[2 * b + 1]["g1"]], axis=2))
        m["x1"] = np.ascontiguousarray(np.concatenate(
            [res3[2 * b]["x1"], res3[2 * b + 1]["x1"]], axis=0))
        in_maps.append(m)
    return in_maps


def kernel(**inputs):
    inputs = {k: np.asarray(v) for k, v in inputs.items()}
    res1 = run_bass_kernel_spmd(build_launch1(), stage_launch1(inputs),
                                core_ids=list(range(NCORES))).results
    res2 = run_bass_kernel_spmd(build_launch2(), stage_launch2(inputs, res1),
                                core_ids=list(range(NCORES))).results
    res3 = run_bass_kernel_spmd(build_launch3(), stage_launch3(inputs, res1, res2),
                                core_ids=list(range(NCORES))).results
    res4 = run_bass_kernel_spmd(build_launch4(), stage_launch4(inputs, res3),
                                core_ids=list(range(4))).results
    x2 = np.stack([np.asarray(r["x2"]) for r in res4], axis=0)
    return x2.astype(np.float32)
